# version 02
# baseline (speedup 1.0000x reference)
import numpy as np
import ml_dtypes
import concourse.bass as bass
import concourse.mybir as mybir
from concourse.bass_utils import run_bass_kernel_spmd
from contextlib import ExitStack

F32 = mybir.dt.float32
BF16 = mybir.dt.bfloat16
AF = mybir.ActivationFunctionType
ALU = mybir.AluOpType
AX = mybir.AxisListType

T = 2048
D = 1024
NT = T // 128
NCH = D // 128

MODE = "layers"
DBG_DUMP = []
SKIP_ATTN = False



class Rec:
    __slots__ = ("meth", "args", "kwargs", "incs")

    def __init__(self, meth, args, kwargs):
        self.meth, self.args, self.kwargs = meth, args, kwargs
        self.incs = []


ENGS = ("pe", "act", "dve", "pool", "sp")


class Ctx:
    def __init__(self, nc):
        self.nc = nc
        self.ops = {e: [] for e in ENGS}
        self.cnt = {"pe": 0, "act": 0, "dve": 0, "pool": 0, "dg": 0, "ds": 0}
        self.lastw = {e: {} for e in ENGS}
        self.sems = {}
        self.hbm_cache = {}
        self.es = ExitStack()

    def op(self, eng, meth, *args, **kwargs):
        r = Rec(meth, args, kwargs)
        self.ops[eng].append(r)
        return r

    def inc(self, rec, sem, k=1):
        rec.incs.append((sem, k))
        self.cnt[sem] += k
        return self.cnt[sem]

    def w(self, eng, sem, val):
        if val is None or val <= 0:
            return
        if self.lastw[eng].get(sem, 0) >= val:
            return
        self.lastw[eng][sem] = val
        self.ops[eng].append(Rec("wait_ge", (sem, val), {}))

    def wtok(self, eng, tok):
        if tok:
            for s, v in tok.items():
                self.w(eng, s, v)

    def snap(self):
        return dict(self.cnt)

    def guard_all(self):
        snap = self.snap()
        for e in ENGS:
            for sem in ("pe", "act", "dve", "pool", "dg", "ds"):
                self.w(e, sem, snap[sem])

    def sop(self, eng, meth, *args, deps=None, **kwargs):
        self.w(eng, eng, self.cnt[eng])
        for tok in (deps or []):
            self.wtok(eng, tok)
        r = self.op(eng, meth, *args, **kwargs)
        c = self.inc(r, eng)
        return {eng: c}

    def dma(self, queue, out, in_, wait=None, **kw):
        sem = "dg" if queue == "pool" else "ds"
        self.wtok(queue, wait)
        r = self.op(queue, "dma_start", out=out, in_=in_, **kw)
        return self.inc(r, sem, 16)

    def hbm(self, name, shape, dtype=F32, kind="ExternalInput"):
        if name not in self.hbm_cache:
            self.hbm_cache[name] = self.nc.dram_tensor(
                name, list(shape), dtype, kind=kind).ap()
        return self.hbm_cache[name]

    def scratch_hbm(self, name, shape, dtype):
        if name not in self.hbm_cache:
            self.hbm_cache[name] = self.nc.dram_tensor(
                name, list(shape), dtype, kind="Internal").ap()
        return self.hbm_cache[name]

    def dbg(self, name, sb_ap, tok):
        if name in DBG_DUMP:
            shape = list(sb_ap.shape)
            dt = sb_ap.dtype
            out = self.hbm("dbg_" + name, shape, dt, kind="ExternalOutput")
            self.dma("sp", out, sb_ap, wait=tok)

    def sb(self, es, name, shape, dtype):
        self._uid = getattr(self, "_uid", 0) + 1
        return es.enter_context(
            self.nc.sbuf_tensor(f"{name}_{self._uid}", list(shape), dtype))

    def ps(self, es, name, shape, dtype=F32):
        self._uid = getattr(self, "_uid", 0) + 1
        return es.enter_context(
            self.nc.psum_tensor(f"{name}_{self._uid}", list(shape), dtype))


class Ring:

    def __init__(self, ctx, tiles, init_tok=None):
        self.ctx = ctx
        self.tiles = list(tiles)
        self.free = [dict(init_tok or {}) for _ in self.tiles]
        self.i = -1

    def next(self):
        self.i += 1
        k = self.i % len(self.tiles)
        return k, self.tiles[k], self.free[k]

    def set_free(self, k, tok):
        self.free[k] = dict(tok)


class TrStream:

    def __init__(self, ctx, ring, ident_bf, ident_f32, eng="dve"):
        self.ctx, self.ring, self.eng = ctx, ring, eng
        self.identb, self.identf = ident_bf, ident_f32
        self.pend = None
        self.last = {}

    def push(self, in_ap, out_ap, wait_tok=None, scalar=None):
        ctx = self.ctx
        npart, ncol = in_ap.shape[0], in_ap.shape[1]
        k, pt, fr = self.ring.next()
        ctx.wtok("pe", fr)
        if wait_tok:
            ctx.wtok("pe", wait_tok)
        ident = self.identb if pt.dtype == BF16 else self.identf
        r = ctx.op("pe", "transpose", pt[0:ncol, 0:npart], in_ap,
                   ident[0:npart, 0:npart])
        c = ctx.inc(r, "pe")
        self._emit_pend(c)
        self.pend = (k, pt[0:ncol, 0:npart], out_ap, scalar)
        return c

    def _emit_pend(self, cover):
        if self.pend is None:
            return
        ctx = self.ctx
        k, pt_ap, out_ap, scalar = self.pend
        ctx.w(self.eng, "pe", cover)
        if scalar is None:
            r = ctx.op(self.eng, "tensor_copy", out_ap, pt_ap)
        else:
            r = ctx.op(self.eng, "tensor_scalar", out_ap, pt_ap, scalar[0],
                       scalar[1], scalar[2], scalar[3])
        c = ctx.inc(r, self.eng)
        self.ring.set_free(k, {self.eng: c})
        self.last = {self.eng: c}
        self.pend = None

    def flush(self, dummy_in):
        ctx = self.ctx
        k, pt, fr = self.ring.next()
        ctx.wtok("pe", fr)
        npart, ncol = dummy_in.shape[0], dummy_in.shape[1]
        ident = self.identb if pt.dtype == BF16 else self.identf
        r = ctx.op("pe", "transpose", pt[0:ncol, 0:npart], dummy_in,
                   ident[0:npart, 0:npart])
        c = ctx.inc(r, "pe")
        self.ring.set_free(k, {"pe": c})
        self._emit_pend(c)
        return self.last


class Lag:

    def __init__(self):
        self.fn = None

    def step(self, cover, fn=None):
        if self.fn is not None:
            self.fn(cover)
        self.fn = fn

    def flush(self, cover):
        if self.fn is not None:
            self.fn(cover)
            self.fn = None


def col_ap(t_ap, n):
    return bass.AP(t_ap.tensor, t_ap.offset, [[1, n], [1, 1]])


def row_ap(t_ap, n):
    return bass.AP(t_ap.tensor, t_ap.offset, [[n, 1], [1, n]])



class Consts:
    pass


def emit_consts(ctx, es, need_l2=False):
    c = Consts()
    c.ident_bf = ctx.sb(es, "ident_bf", [128, 128], BF16)
    c.ident_f32 = ctx.sb(es, "ident_f32", [128, 128], F32)
    c.tri = ctx.sb(es, "tri", [128, 128], BF16)
    c.ones_bf = ctx.sb(es, "ones_bf", [1, 512], BF16)
    c.ones_hi = ctx.sb(es, "ones_hi", [65, 64], F32)
    c.zeros16 = ctx.sb(es, "zeros16", [128, 16], F32)
    c.eps_col = ctx.sb(es, "eps_col", [128, 1], F32)
    if need_l2:
        c.jrev = ctx.sb(es, "jrev", [128, 128], BF16)

    for t, cmp, fill, base in ((c.ident_bf, ALU.not_equal, 1.0, 0),
                               (c.ident_f32, ALU.not_equal, 1.0, 0)):
        ctx.op("pool", "memset", t[:, :], 0.0)
        ctx.op("pool", "affine_select", out=t[:, :], in_=t[:, :],
               compare_op=cmp, fill=fill, base=base,
               pattern=[[-1, 128]], channel_multiplier=1)
    ctx.op("pool", "memset", c.tri[:, :], 1.0)
    ctx.op("pool", "affine_select", out=c.tri[:, :], in_=c.tri[:, :],
           compare_op=ALU.is_ge, fill=0.0, base=0,
           pattern=[[1, 128]], channel_multiplier=-1)
    ctx.op("pool", "memset", c.ones_bf[:, :], 1.0)
    ctx.op("pool", "memset", c.ones_hi[:, :], 1.0)
    ctx.op("pool", "memset", c.zeros16[:, :], 0.0)
    ctx.op("pool", "memset", c.eps_col[:, :], 1e-5)
    if need_l2:
        ctx.op("pool", "memset", c.jrev[:, :], 0.0)
        r = ctx.op("pool", "affine_select", out=c.jrev[:, :], in_=c.jrev[:, :],
                   compare_op=ALU.not_equal, fill=1.0, base=-127,
                   pattern=[[1, 128]], channel_multiplier=1)
    else:
        r = ctx.op("pool", "memset", c.zeros16[:, :], 0.0)
    c.ready = {"pool": ctx.inc(r, "pool")}
    return c



def emit_bias_cols(ctx, es, name, hbm_vec, n, psr, con, tok=None):
    nr = n // 128
    rows = ctx.sb(es, name + "_r", [nr, 128], F32)
    cols = ctx.sb(es, name + "_c", [128, nr], F32)
    src = bass.AP(hbm_vec.tensor, hbm_vec.offset, [[128, nr], [1, 128]])
    dcnt = ctx.dma("pool", rows[:, :], src, wait=tok)
    ctx.w("pe", "dg", dcnt)
    ctx.wtok("pe", con.ready)
    trs = TrStream(ctx, psr, con.ident_bf, con.ident_f32)
    trs.push(rows[:, :], cols[:, :])
    return cols, trs.flush(con.ident_f32[0:nr, :])



def emit_ln(ctx, es, x_sb, xn, con, x_tok):
    ctx.guard_all()
    les = ExitStack()
    sq = ctx.sb(les, "ln_sq", [128, 1024], F32)
    st = ctx.sb(les, "ln_st", [128, 2, 8], F32)
    xt = ctx.sb(les, "ln_xt", [128, 2, 1024], BF16)
    pt = [ctx.ps(les, f"ln_pt{i}", [128, 128], BF16) for i in range(2)]
    ptr = Ring(ctx, pt, ctx.snap())
    trs = TrStream(ctx, ptr, con.ident_bf, con.ident_f32)
    act_done = [0, 0]
    pe_done = [0, 0]
    for t in range(NT):
        p = t % 2
        s = st[:, p, :]
        xa = x_sb[:, t, :]
        ctx.wtok("dve", x_tok)
        ctx.w("dve", "act", act_done[p])
        t1 = ctx.sop("dve", "tensor_reduce", s[:, 0:1], xa, AX.X, ALU.add)
        ctx.wtok("act", x_tok)
        t2 = ctx.sop("act", "activation", sq[:, :], xa, AF.Square,
                     accum_out=s[:, 1:2])
        t3 = ctx.sop("dve", "tensor_scalar", s[:, 2:3], s[:, 0:1], 1.0 / D,
                     None, ALU.mult, deps=[t1])
        t4 = ctx.sop("act", "activation", s[:, 3:4], s[:, 2:3], AF.Square,
                     deps=[t3])
        t5 = ctx.sop("dve", "scalar_tensor_tensor", s[:, 4:5], s[:, 1:2],
                     1.0 / D, s[:, 3:4], ALU.mult, ALU.subtract,
                     deps=[t2, t4])
        t6 = ctx.sop("act", "activation", s[:, 5:6], s[:, 4:5], AF.Ln,
                     bias=con.eps_col[:, 0:1], deps=[t5])
        t7 = ctx.sop("act", "activation", s[:, 6:7], s[:, 5:6], AF.Exp,
                     scale=-0.5, deps=[t6])
        t8 = ctx.sop("dve", "scalar_tensor_tensor", s[:, 7:8], s[:, 2:3], -1.0,
                     s[:, 6:7], ALU.mult, ALU.mult, deps=[t7])
        ctx.wtok("act", t8)
        ctx.w("act", "pe", pe_done[p])
        r = ctx.op("act", "activation", xt[:, p, :], xa, AF.Identity,
                   bias=s[:, 7:8], scale=s[:, 6:7])
        c_xt = ctx.inc(r, "act")
        act_done[p] = c_xt
        ctx.w("pe", "act", c_xt)
        ctx.wtok("pe", con.ready)
        for c in range(NCH):
            c_tr = trs.push(xt[:, p, c * 128:(c + 1) * 128],
                            xn[:, c, t * 128:(t + 1) * 128])
        pe_done[p] = c_tr
    tok = trs.flush(con.ident_bf[:, :])
    les.close()
    return tok



def emit_logits_tok(ctx, es, wcols_sb, nE, xn, xn_tok, con, psr2, bias_col=None,
                    name="g", func=None):
    ctx.guard_all()
    glog = ctx.sb(es, name + "_log", [nE, T], F32)
    gtok = ctx.sb(es, name + "_tok", [128, NT, nE], F32)
    pes = ExitStack()
    psg = [ctx.ps(pes, f"{name}_ps{i}", [128, 512], F32) for i in range(2)]
    psd = ctx.ps(pes, f"{name}_psd", [128, 512], F32)
    pst = [ctx.ps(pes, f"{name}_pt{i}", [128, nE], F32) for i in range(2)]
    gr = Ring(ctx, psg, ctx.snap())
    tr = Ring(ctx, pst, ctx.snap())
    for tt in range(4):
        k, pg, fr = gr.next()
        ctx.wtok("pe", fr)
        ctx.wtok("pe", xn_tok)
        for c in range(NCH):
            r = ctx.op("pe", "matmul", pg[0:nE, :], wcols_sb[:, c, :],
                       xn[:, c, tt * 512:(tt + 1) * 512],
                       start=(c == 0), stop=(c == NCH - 1))
        ctx.inc(r, "pe")
        r = ctx.op("pe", "matmul", psd[:, :], xn[:, 0, 0:128],
                   xn[:, 0, 0:512], start=True, stop=True)
        c_mm = ctx.inc(r, "pe")
        ctx.w("act", "pe", c_mm)
        kw = {}
        if bias_col is not None:
            kw["bias"] = bias_col
        r = ctx.op("act", "activation", glog[:, tt * 512:(tt + 1) * 512],
                   pg[0:nE, :], func if func is not None else AF.Identity,
                   **kw)
        c_ev = ctx.inc(r, "act")
        gr.set_free(k, {"act": c_ev})
    trs = TrStream(ctx, tr, con.ident_bf, con.ident_f32)
    for t in range(NT):
        ctx.w("pe", "act", c_ev)
        trs.push(glog[0:nE, t * 128:(t + 1) * 128], gtok[:, t, :])
    tok = trs.flush(con.ident_f32[0:nE, :])
    pes.close()
    return gtok, tok



def emit_ffn_dense(ctx, es, P, x_sb, xn, xn_tok, x_tok, con):
    ctx.guard_all()
    fes = ExitStack()
    start = ctx.snap()

    gws = ctx.sb(fes, "gd_w", [128, NCH, 4], BF16)
    ctx.dma("pool", gws[:, :, :],
            P["gate_w"].rearrange("(c p) e -> p c e", p=128),
            wait={"dve": start["dve"], "act": start["act"], "pe": start["pe"]})
    gbc = ctx.sb(fes, "gd_b", [4, 1], F32)
    dcnt = ctx.dma("pool", gbc[:, :], col_ap(P["gate_b"], 4))
    ctx.w("pe", "dg", dcnt)
    gtok, gt_tok = emit_logits_tok(ctx, fes, gws, 4, xn, xn_tok, con, None,
                                   bias_col=gbc[:, 0:1], name="gd")
    wt = ctx.sb(fes, "gd_wt", [128, NT, 4], F32)
    sc = ctx.sb(fes, "gd_sc", [128, 12], F32)
    esm = ctx.sb(fes, "gd_esm", [128, 2, 4], F32)
    wt_tok = None
    for t in range(NT):
        p = t % 2
        g = gtok[:, t, :]
        t0 = ctx.sop("dve", "tensor_reduce", sc[:, 0:1], g, AX.X, ALU.max,
                     negate=True, deps=[gt_tok])
        t1 = ctx.sop("act", "activation", esm[:, p, :], g, AF.Exp,
                     bias=sc[:, 0:1], accum_out=sc[:, 2:3], deps=[t0, gt_tok])
        t2 = ctx.sop("dve", "reciprocal", sc[:, 3:4], sc[:, 2:3], deps=[t1])
        t3 = ctx.sop("dve", "tensor_scalar", esm[:, p, :], esm[:, p, :],
                     sc[:, 3:4], None, ALU.mult, deps=[t2])
        t4 = ctx.sop("dve", "tensor_reduce", sc[:, 4:5], esm[:, p, :], AX.X,
                     ALU.max, deps=[t3])
        t5 = ctx.sop("dve", "tensor_scalar", sc[:, 5:9], esm[:, p, :],
                     sc[:, 4:5], None, ALU.is_ge, deps=[t4])
        t6 = ctx.sop("dve", "scalar_tensor_tensor", sc[:, 5:9], sc[:, 5:9],
                     -1e30, esm[:, p, :], ALU.mult, ALU.add, deps=[t5])
        t7 = ctx.sop("dve", "tensor_reduce", sc[:, 9:10], sc[:, 5:9], AX.X,
                     ALU.max, deps=[t6])
        t8 = ctx.sop("dve", "tensor_scalar", sc[:, 5:9], esm[:, p, :],
                     sc[:, 9:10], None, ALU.is_ge, deps=[t7])
        tw = ctx.sop("dve", "scalar_tensor_tensor", wt[:, t, :], sc[:, 5:9],
                     0.5, esm[:, p, :], ALU.mult, ALU.mult, deps=[t8])
        wt_tok = tw

    FBD = 512
    NFD = FBD // 128
    h = ctx.sb(fes, "fd_h", [128, NFD, T], BF16)
    w1r = [ctx.sb(fes, f"fd_w1{i}", [128, NCH, FBD], BF16) for i in range(2)]
    w2r = [ctx.sb(fes, f"fd_w2{i}", [128, NFD, 1024], BF16) for i in range(2)]
    b2row = [ctx.sb(fes, f"fd_b2{i}", [1, 1024], BF16) for i in range(2)]
    b1c = [None, None]
    pes = ExitStack()
    psA = [ctx.ps(pes, f"fd_pa{i}", [128, 512], F32) for i in range(3)]
    psB = [ctx.ps(pes, f"fd_pb{i}", [128, 512], F32) for i in range(3)]
    psb = [ctx.ps(pes, f"fd_pc{i}", [128, 128], F32) for i in range(2)]
    rA = Ring(ctx, psA, start)
    rB = Ring(ctx, psB, start)
    rC = Ring(ctx, psb, start)
    w1ring = Ring(ctx, w1r, start)
    w2ring = Ring(ctx, w2r, start)
    b2ring = Ring(ctx, b2row, start)

    exps = []
    for ns in range(2):
        exps.append((P["sw1"][ns], P["sb1"][ns], P["sw2"][ns], P["sb2"][ns],
                     ("const", 0.25)))
    for e in range(4):
        exps.append((P["rw1"][e], P["rb1"][e], P["rw2"][e], P["rb2"][e],
                     ("wt", e)))

    NFB = 4096 // FBD
    x_last = dict(x_tok)
    xl_box = [dict(x_tok)]
    blag = Lag()
    for ei, (w1h, b1h, w2h, b2h, coeff) in enumerate(exps):
        ep = ei % 2
        bes = ExitStack()
        b1c_e, b1tok = emit_bias_cols(ctx, bes, f"fd_b1_{ei}", b1h, 4096, rC,
                                      con, tok={"act": ctx.cnt["act"], "pe": ctx.cnt["pe"]})
        kb, b2t, fr = b2ring.next()
        c_b2 = ctx.dma("pool", b2t[:, :], row_ap(b2h, 1024), wait=fr)
        for fb in range(NFB):
            k1, w1t, fr1 = w1ring.next()
            c_w1 = ctx.dma(
                "pool", w1t[:, :, :],
                w1h.rearrange("(c p) f -> p c f", p=128)
                [:, :, fb * FBD:(fb + 1) * FBD], wait=fr1)
            k2, w2t, fr2 = w2ring.next()
            c_w2 = ctx.dma(
                "pool", w2t[:, :, :],
                w2h[fb * FBD:(fb + 1) * FBD, :]
                .rearrange("(c p) o -> p c o", p=128), wait=fr2)
            ctx.w("pe", "dg", c_w1)
            ctx.wtok("pe", xn_tok)
            h_tok = None
            for f in range(NFD):
                for tt in range(4):
                    k, pa, fra = rA.next()
                    ctx.wtok("pe", fra)
                    for c in range(NCH):
                        r = ctx.op("pe", "matmul", pa[:, :],
                                   w1t[:, c, f * 128:(f + 1) * 128],
                                   xn[:, c, tt * 512:(tt + 1) * 512],
                                   start=(c == 0), stop=(c == NCH - 1))
                    c_mm = ctx.inc(r, "pe")
                    ctx.w("act", "pe", c_mm)
                    ctx.wtok("act", b1tok)
                    r = ctx.op("act", "activation",
                               h[:, f, tt * 512:(tt + 1) * 512], pa[:, :],
                               AF.Relu,
                               bias=b1c_e[:, fb * NFD + f:fb * NFD + f + 1])
                    c_ev = ctx.inc(r, "act")
                    rA.set_free(k, {"act": c_ev})
                    h_tok = c_ev
            ctx.w("pe", "dg", c_w2)
            if ei == 0 and fb == 0:
                ctx.w("pe", "dg", c_b2)
            ctx.w("pe", "act", h_tok)
            for tt in range(NT):
                for do in range(2):
                    k, pb, frb = rB.next()
                    ctx.wtok("pe", frb)
                    for f in range(NFD):
                        r = ctx.op("pe", "matmul", pb[:, :],
                                   h[:, f, tt * 128:(tt + 1) * 128],
                                   w2t[:, f, do * 512:(do + 1) * 512],
                                   start=(f == 0),
                                   stop=(f == NFD - 1 and fb != 0))
                    if fb == 0:
                        ctx.w("pe", "dg", c_b2)
                        r = ctx.op("pe", "matmul", pb[:, :],
                                   con.ones_bf[0:1, 0:128],
                                   b2t[0:1, do * 512:(do + 1) * 512],
                                   start=False, stop=True)
                    c_mm = ctx.inc(r, "pe")

                    def ev(cover, k=k, pb=pb, tt=tt, do=do, coeff=coeff):
                        ctx.w("dve", "pe", cover)
                        ctx.wtok("dve", wt_tok)
                        if coeff[0] == "const":
                            s1 = coeff[1]
                        else:
                            s1 = wt[:, tt, coeff[1]:coeff[1] + 1]
                        xs = x_sb[:, tt, do * 512:(do + 1) * 512]
                        r2 = ctx.op("dve", "scalar_tensor_tensor", xs,
                                    pb[:, :], s1, xs, ALU.mult, ALU.add)
                        c_x = ctx.inc(r2, "dve")
                        rB.set_free(k, {"dve": c_x})
                        xl_box[0] = {"dve": c_x}
                    blag.step(c_mm, ev)
            w1ring.set_free(k1, {"pe": ctx.cnt["pe"]})
            w2ring.set_free(k2, {"pe": ctx.cnt["pe"]})
        b2ring.set_free(kb, {"pe": ctx.cnt["pe"]})
        bes.close()
    k, pb, frb = rB.next()
    ctx.wtok("pe", frb)
    r = ctx.op("pe", "matmul", pb[:, :], h[:, 0, 0:128], w2r[0][:, 0, 0:512],
               start=True, stop=True)
    c_cov = ctx.inc(r, "pe")
    rB.set_free(k, {"pe": c_cov})
    blag.flush(c_cov)
    x_last = xl_box[0]
    pes.close()
    fes.close()
    return x_last



def emit_ffn_l1(ctx, es, P, x_sb, xn, xn_tok, x_tok, con):
    fes = ExitStack()
    ctx.guard_all()
    start = ctx.snap()

    gws = ctx.sb(fes, "g1_w", [128, NCH, 8], BF16)
    ctx.dma("pool", gws[:, :, :],
            P["Wg"].rearrange("(c p) e -> p c e", p=128))
    gtok, gt_tok = emit_logits_tok(ctx, fes, gws, 8, xn, xn_tok, con, None,
                                   name="g1", func=AF.Sigmoid)
    wt = ctx.sb(fes, "g1_wt", [128, NT, 8], F32)
    sc = ctx.sb(fes, "g1_sc", [128, 4], F32)
    va = ctx.sb(fes, "g1_va", [128, 2, 8], F32)
    wt_tok = None
    for t in range(NT):
        g = gtok[:, t, :]
        tk = ctx.sop("dve", "tensor_reduce", sc[:, 0:1], g, AX.X, ALU.max,
                     deps=[gt_tok])
        tk = ctx.sop("dve", "tensor_scalar", va[:, 0, :], g, sc[:, 0:1], None,
                     ALU.is_ge, deps=[tk])
        tk = ctx.sop("dve", "scalar_tensor_tensor", va[:, 1, :], va[:, 0, :],
                     -1e30, g, ALU.mult, ALU.add, deps=[tk])
        tk = ctx.sop("dve", "tensor_reduce", sc[:, 1:2], va[:, 1, :], AX.X,
                     ALU.max, deps=[tk])
        tk = ctx.sop("dve", "tensor_scalar", va[:, 0, :], g, sc[:, 1:2], None,
                     ALU.is_ge, deps=[tk])
        tk = ctx.sop("dve", "tensor_mul", va[:, 1, :], g, va[:, 0, :],
                     deps=[tk])
        tk = ctx.sop("dve", "tensor_reduce", sc[:, 2:3], va[:, 1, :], AX.X,
                     ALU.add, deps=[tk])
        tk = ctx.sop("dve", "reciprocal", sc[:, 3:4], sc[:, 2:3], deps=[tk])
        tk = ctx.sop("dve", "tensor_scalar", wt[:, t, :], va[:, 1, :],
                     sc[:, 3:4], 1.0 / 6.0, ALU.mult, ALU.mult, deps=[tk])
        wt_tok = tk

    FB = 512
    NFB = 2048 // FB
    NF = FB // 128
    h = ctx.sb(fes, "f1_h", [128, NF, T], BF16)
    w1a = [ctx.sb(fes, f"f1_wa{i}", [128, NCH, FB], BF16) for i in range(2)]
    w1b = [ctx.sb(fes, f"f1_wb{i}", [128, NCH, FB], BF16) for i in range(2)]
    w2r = [ctx.sb(fes, f"f1_w2{i}", [128, NF, 1024], BF16) for i in range(2)]
    s1t = ctx.sb(fes, "f1_s1", [128, 2, 512], BF16)
    pes = ExitStack()
    psZ = [ctx.ps(pes, f"f1_pz{i}", [128, 512], F32) for i in range(4)]
    psB = [ctx.ps(pes, f"f1_pb{i}", [128, 512], F32) for i in range(3)]
    rZa = Ring(ctx, psZ[0:2], start)
    rZb = Ring(ctx, psZ[2:4], start)
    rB = Ring(ctx, psB, start)
    r1a = Ring(ctx, w1a, start)
    r1b = Ring(ctx, w1b, start)
    r2 = Ring(ctx, w2r, start)
    rS = Ring(ctx, [s1t[:, 0, :], s1t[:, 1, :]], start)

    exps = [(P["Ws1"], P["Ws3"], P["Ws2"], ("const", 1.0 / 6.0))]
    for e in range(8):
        exps.append((P["We1"][e], P["We3"][e], P["We2"][e], ("wt", e)))

    x_last = dict(x_tok)
    xl_box = [dict(x_tok)]
    blag = Lag()
    for ei, (w1h, w3h, w2h, coeff) in enumerate(exps):
        for fb in range(NFB):
            fsl = slice(fb * FB, (fb + 1) * FB)
            ka, wta_, fra = r1a.next()
            c_wa = ctx.dma("pool", wta_[:, :, :],
                           w1h.rearrange("(c p) f -> p c f", p=128)[:, :, fsl],
                           wait=fra)
            kb, wtb_, frb = r1b.next()
            c_wb = ctx.dma("pool", wtb_[:, :, :],
                           w3h.rearrange("(c p) f -> p c f", p=128)[:, :, fsl],
                           wait=frb)
            k2, wt2_, fr2 = r2.next()
            c_w2 = ctx.dma("pool", wt2_[:, :, :],
                           w2h[fsl, :].rearrange("(c p) o -> p c o", p=128),
                           wait=fr2)
            ctx.w("pe", "dg", c_wb)
            ctx.wtok("pe", xn_tok)
            h_tok = None
            for f in range(NF):
                for tt in range(4):
                    kz1, pz1, frz1 = rZa.next()
                    ctx.wtok("pe", frz1)
                    for c in range(NCH):
                        r = ctx.op("pe", "matmul", pz1[:, :],
                                   wta_[:, c, f * 128:(f + 1) * 128],
                                   xn[:, c, tt * 512:(tt + 1) * 512],
                                   start=(c == 0), stop=(c == NCH - 1))
                    c_m1 = ctx.inc(r, "pe")
                    kz3, pz3, frz3 = rZb.next()
                    ctx.wtok("pe", frz3)
                    for c in range(NCH):
                        r = ctx.op("pe", "matmul", pz3[:, :],
                                   wtb_[:, c, f * 128:(f + 1) * 128],
                                   xn[:, c, tt * 512:(tt + 1) * 512],
                                   start=(c == 0), stop=(c == NCH - 1))
                    c_m3 = ctx.inc(r, "pe")
                    ks, s1s, frs = rS.next()
                    ctx.w("act", "pe", c_m1)
                    ctx.wtok("act", frs)
                    r = ctx.op("act", "activation", s1s, pz1[:, :], AF.Silu)
                    c_s1 = ctx.inc(r, "act")
                    rZa.set_free(kz1, {"act": c_s1})
                    ctx.w("dve", "act", c_s1)
                    ctx.w("dve", "pe", c_m3)
                    r = ctx.op("dve", "tensor_tensor",
                               h[:, f, tt * 512:(tt + 1) * 512], s1s,
                               pz3[:, :], ALU.mult)
                    c_h = ctx.inc(r, "dve")
                    rZb.set_free(kz3, {"dve": c_h})
                    rS.set_free(ks, {"dve": c_h})
                    h_tok = c_h
            ctx.w("pe", "dg", c_w2)
            ctx.w("pe", "dve", h_tok)
            for tt in range(NT):
                for do in range(2):
                    k, pb, frb2 = rB.next()
                    ctx.wtok("pe", frb2)
                    for f in range(NF):
                        r = ctx.op("pe", "matmul", pb[:, :],
                                   h[:, f, tt * 128:(tt + 1) * 128],
                                   wt2_[:, f, do * 512:(do + 1) * 512],
                                   start=(f == 0), stop=(f == NF - 1))
                    c_mm = ctx.inc(r, "pe")

                    def ev(cover, k=k, pb=pb, tt=tt, do=do, coeff=coeff):
                        ctx.w("dve", "pe", cover)
                        ctx.wtok("dve", wt_tok)
                        if coeff[0] == "const":
                            s1 = coeff[1]
                        else:
                            s1 = wt[:, tt, coeff[1]:coeff[1] + 1]
                        xs = x_sb[:, tt, do * 512:(do + 1) * 512]
                        r2 = ctx.op("dve", "scalar_tensor_tensor", xs,
                                    pb[:, :], s1, xs, ALU.mult, ALU.add)
                        c_x = ctx.inc(r2, "dve")
                        rB.set_free(k, {"dve": c_x})
                        xl_box[0] = {"dve": c_x}
                    blag.step(c_mm, ev)
            r1a.set_free(ka, {"pe": ctx.cnt["pe"]})
            r1b.set_free(kb, {"pe": ctx.cnt["pe"]})
            r2.set_free(k2, {"pe": ctx.cnt["pe"]})
    k, pb, frb2 = rB.next()
    ctx.wtok("pe", frb2)
    r = ctx.op("pe", "matmul", pb[:, :], h[:, 0, 0:128],
               w2r[0][:, 0, 0:512], start=True, stop=True)
    c_cov = ctx.inc(r, "pe")
    rB.set_free(k, {"pe": c_cov})
    blag.flush(c_cov)
    x_last = xl_box[0]
    pes.close()
    fes.close()
    return x_last



def emit_ffn_l3(ctx, es, P, x_sb, xn, xn_tok, x_tok, con, staged):
    ctx.guard_all()
    fes = ExitStack()
    start = ctx.snap()
    rts = ctx.sb(fes, "l3_rt", [128, NCH, 16], BF16)
    w0 = {"dve": start["dve"], "act": start["act"], "pe": start["pe"]}
    w0.update(staged["l3_routerT_tok"])
    ctx.dma("sp", rts[:, :, :],
            staged["l3_routerT"].rearrange("(c p) e -> p c e", p=128),
            wait=w0)
    rbc = ctx.sb(fes, "l3_rb", [16, 1], F32)
    dcnt = ctx.dma("pool", rbc[:, :], col_ap(P["router_b"], 16))
    ctx.w("pe", "ds", ctx.cnt["ds"])
    ctx.w("pe", "dg", dcnt)
    ctx.dbg("l3_rts", rts[:, :, :], {"ds": ctx.cnt["ds"]})
    ltok, lt_tok = emit_logits_tok(ctx, fes, rts, 16, xn, xn_tok, con, None,
                                   bias_col=rbc[:, 0:1], name="l3g")
    ptil = ctx.sb(fes, "l3_pt", [128, NT, 16], F32)
    ccol = ctx.sb(fes, "l3_cc", [128, NT], F32)
    sc = ctx.sb(fes, "l3_sc", [128, 8], F32)
    va = ctx.sb(fes, "l3_va", [128, 2, 16], F32)
    psm = ctx.sb(fes, "l3_psm", [128, 16], F32)
    pt_tok = None
    for t in range(NT):
        g = ltok[:, t, :]
        tk = ctx.sop("dve", "tensor_reduce", sc[:, 0:1], g, AX.X, ALU.max,
                     negate=True, deps=[lt_tok])
        tk = ctx.sop("act", "activation", psm[:, :], g, AF.Exp,
                     bias=sc[:, 0:1], accum_out=sc[:, 2:3], deps=[tk, lt_tok])
        tk = ctx.sop("dve", "reciprocal", sc[:, 3:4], sc[:, 2:3], deps=[tk])
        tk = ctx.sop("dve", "tensor_scalar", psm[:, :], psm[:, :], sc[:, 3:4],
                     None, ALU.mult, deps=[tk])
        tk = ctx.sop("dve", "tensor_copy", va[:, 0, :], psm[:, :], deps=[tk])
        pv = 0
        for it in range(5):
            tk = ctx.sop("dve", "tensor_reduce", sc[:, 4:5], va[:, pv, :],
                         AX.X, ALU.max, deps=[tk])
            tk = ctx.sop("dve", "tensor_scalar", va[:, 1 - pv, :],
                         va[:, pv, :], sc[:, 4:5], None, ALU.is_ge, deps=[tk])
            tk = ctx.sop("dve", "scalar_tensor_tensor", va[:, 1 - pv, :],
                         va[:, 1 - pv, :], -1e30, va[:, pv, :], ALU.mult,
                         ALU.add, deps=[tk])
            pv = 1 - pv
        tk = ctx.sop("dve", "tensor_reduce", sc[:, 4:5], va[:, pv, :], AX.X,
                     ALU.max, deps=[tk])
        tk = ctx.sop("dve", "tensor_scalar", va[:, 1 - pv, :], psm[:, :],
                     sc[:, 4:5], None, ALU.is_ge, deps=[tk])
        tk = ctx.sop("dve", "tensor_mul", va[:, pv, :], psm[:, :],
                     va[:, 1 - pv, :], deps=[tk])
        tk = ctx.sop("dve", "tensor_reduce", sc[:, 5:6], va[:, pv, :], AX.X,
                     ALU.add, deps=[tk])
        tk = ctx.sop("dve", "reciprocal", sc[:, 6:7], sc[:, 5:6], deps=[tk])
        tk = ctx.sop("dve", "tensor_scalar", ptil[:, t, :], va[:, pv, :],
                     sc[:, 6:7], 0.5, ALU.mult, ALU.mult, deps=[tk])
        tk = ctx.sop("dve", "tensor_scalar", sc[:, 7:8], sc[:, 5:6], 1e-10,
                     None, ALU.add, deps=[tk])
        tk = ctx.sop("dve", "reciprocal", sc[:, 4:5], sc[:, 7:8], deps=[tk])
        tk = ctx.sop("dve", "scalar_tensor_tensor", ccol[:, t:t + 1],
                     sc[:, 5:6], 0.5e-10, sc[:, 4:5], ALU.mult, ALU.mult,
                     deps=[tk])
        pt_tok = tk
    ctx.dbg("l3_ltok", ltok[:, :, :], pt_tok)
    ctx.dbg("l3_ptil", ptil[:, :, :], pt_tok)

    wr = [ctx.sb(fes, f"l3_w{i}", [128, NCH, 1024], BF16) for i in range(2)]
    pes = ExitStack()
    psB = [ctx.ps(pes, f"l3_pb{i}", [128, 512], F32) for i in range(4)]
    rB = Ring(ctx, psB, start)
    wring = Ring(ctx, wr, start)
    x_last = dict(x_tok)
    for e in range(16):
        k1, wt1, fr1 = wring.next()
        c_w = ctx.dma("pool", wt1[:, :, :],
                      P["expert_w"][e].rearrange("(c p) o -> p c o", p=128),
                      wait=fr1)
        ctx.w("pe", "dg", c_w)
        ctx.wtok("pe", xn_tok)
        for tt in range(NT):
            for do in range(2):
                k, pb, frb = rB.next()
                ctx.wtok("pe", frb)
                for c in range(NCH):
                    r = ctx.op("pe", "matmul", pb[:, :],
                               xn[:, c, tt * 128:(tt + 1) * 128],
                               wt1[:, c, do * 512:(do + 1) * 512],
                               start=(c == 0), stop=(c == NCH - 1))
                c_mm = ctx.inc(r, "pe")
                ctx.w("dve", "pe", c_mm)
                ctx.wtok("dve", pt_tok)
                xs = x_sb[:, tt, do * 512:(do + 1) * 512]
                r = ctx.op("dve", "scalar_tensor_tensor", xs, pb[:, :],
                           ptil[:, tt, e:e + 1], xs, ALU.mult, ALU.add)
                c_x = ctx.inc(r, "dve")
                rB.set_free(k, {"dve": c_x})
                x_last = {"dve": c_x}
            if e == 15:
                r = ctx.op("dve", "tensor_scalar", x_sb[:, tt, :],
                           x_sb[:, tt, :], ccol[:, tt:tt + 1], None, ALU.add)
                c_x = ctx.inc(r, "dve")
                x_last = {"dve": c_x}
        wring.set_free(k1, {"pe": ctx.cnt["pe"]})
    pes.close()
    fes.close()
    return x_last



def emit_stage_wT(ctx, es, name, w_hbm, O, I, con, staged):
    wT = ctx.scratch_hbm(name + "_s", [I, O], BF16)
    ses = ExitStack()
    ctx.guard_all()
    start = ctx.snap()
    nio = I // 128
    wsb = [ctx.sb(ses, f"stw_i{i}", [128, I], BF16) for i in range(2)]
    wtt = [ctx.sb(ses, f"stw_o{i}", [128, nio, 128], BF16) for i in range(2)]
    pt = [ctx.ps(ses, f"stw_p{i}", [128, 128], BF16) for i in range(2)]
    rin = Ring(ctx, wsb, start)
    rout = Ring(ctx, wtt, start)
    trs = TrStream(ctx, pt and Ring(ctx, pt, start), con.ident_bf,
                   con.ident_f32)
    c_out = 0
    for oc in range(O // 128):
        ki, wi, fri = rin.next()
        c_in = ctx.dma("pool", wi[:, :], w_hbm[oc * 128:(oc + 1) * 128, :],
                       wait=fri)
        ko, wo, fro = rout.next()
        ctx.w("pe", "dg", c_in)
        ctx.wtok("pe", con.ready)
        for ic in range(nio):
            if ic == 0:
                ctx.wtok("dve", fro)
            trs.push(wi[:, ic * 128:(ic + 1) * 128], wo[:, ic, :])
        tok = trs.flush(con.ident_bf[:, :])
        rin.set_free(ki, {"pe": ctx.cnt["pe"]})
        c_out = ctx.dma("sp", wT.rearrange("(c p) o -> p c o", p=128)
                        [:, :, oc * 128:(oc + 1) * 128], wo[:, :, :], wait=tok)
        rout.set_free(ko, {"ds": c_out})
    ses.close()
    staged[name] = wT
    staged[name + "_tok"] = {"ds": c_out}


def emit_stage_l1_qkvT(ctx, es, P, con, staged):
    wT = ctx.scratch_hbm("l1_qkvT_s", [64, 192], BF16)
    ses = ExitStack()
    ctx.guard_all()
    start = ctx.snap()
    wsb = ctx.sb(ses, "stq_i", [64, 192], BF16)
    wo = ctx.sb(ses, "stq_o", [64, 192], BF16)
    pt = [ctx.ps(ses, f"stq_p{i}", [128, 128], BF16) for i in range(2)]
    trs = TrStream(ctx, Ring(ctx, pt, start), con.ident_bf, con.ident_f32)
    c1 = ctx.dma("pool", wsb[:, 0:64], P["Wq_w"])
    c2 = ctx.dma("pool", wsb[:, 64:128], P["Wk_w"])
    c3 = ctx.dma("pool", wsb[:, 128:192], P["Wv_w"])
    ctx.w("pe", "dg", c3)
    ctx.wtok("pe", con.ready)
    for i in range(3):
        trs.push(wsb[:, i * 64:(i + 1) * 64], wo[:, i * 64:(i + 1) * 64])
    tok = trs.flush(con.ident_bf[0:64, :])
    c_out = ctx.dma("sp", wT[:, :], wo[:, :], wait=tok)
    ses.close()
    staged["l1_qkvT"] = wT
    staged["l1_qkvT_tok"] = {"ds": c_out}


def emit_stage_l2_table(ctx, es, P, con, staged):
    vphbuf = ctx.scratch_hbm("l2_vph_s", [16, 2432], F32)
    negmb = ctx.sb(es, "l2_negmb", [128, 16], F32)
    ses = ExitStack()
    ctx.guard_all()
    start = ctx.snap()
    rel_sb = [ctx.sb(ses, f"stt_r{i}", [128, 1024], F32) for i in range(2)]
    red_sb = [ctx.sb(ses, f"stt_d{i}", [128, 16], F32) for i in range(2)]
    full_ts = ctx.sb(ses, "stt_f", [16, 2048], F32)
    vph_sb = ctx.sb(ses, "stt_v", [16, 2432], F32)
    mbcol = ctx.sb(ses, "stt_mc", [16, 1], F32)
    mbrow = ctx.sb(ses, "stt_mr", [1, 16], F32)
    mbrow_bf = ctx.sb(ses, "stt_mb", [1, 16], BF16)
    pt = [ctx.ps(ses, f"stt_p{i}", [128, 128], F32) for i in range(2)]
    psb = ctx.ps(ses, "stt_pb", [128, 32], F32)
    rrel = Ring(ctx, rel_sb, start)
    rred = Ring(ctx, red_sb, start)
    trs = TrStream(ctx, Ring(ctx, pt, start), con.ident_bf, con.ident_f32)
    rel = P["rel_pos_emb"]
    for t in range(16):
        rows = 128 if t < 15 else 127
        ki, ri, fri = rrel.next()
        c_in = ctx.dma("sp", ri[0:rows, :], rel[t * 128:t * 128 + rows, :],
                       wait=fri)
        kd, rd, frd = rred.next()
        ctx.w("dve", "ds", c_in)
        ctx.wtok("dve", frd)
        r = ctx.op("dve", "tensor_reduce", rd[0:rows, :],
                   ri[0:rows, :].rearrange("p (h d) -> p h d", d=64),
                   AX.X, ALU.add)
        c_red = ctx.inc(r, "dve")
        rrel.set_free(ki, {"dve": c_red})
        ctx.w("pe", "dve", c_red)
        ctx.wtok("pe", con.ready)
        c_tr = trs.push(rd[0:rows, :], full_ts[:, t * 128:t * 128 + rows])
        rred.set_free(kd, {"pe": c_tr})
    ttok = trs.flush(con.ident_f32[0:16, :])
    tk = ctx.sop("dve", "tensor_copy", vph_sb[:, 0:1280],
                 full_ts[:, 767:2047], deps=[ttok])
    bcast_src = bass.AP(full_ts, 2046, [[2048, 16], [0, 1152]])
    tk = ctx.sop("dve", "tensor_copy", vph_sb[:, 1280:2432], bcast_src,
                 deps=[tk])
    c_vout = ctx.dma("sp", vphbuf[:, :], vph_sb[:, :], wait=tk)
    tk2 = ctx.sop("dve", "tensor_reduce", mbcol[:, :], full_ts[:, 0:2047],
                  AX.X, ALU.max, negate=True, deps=[ttok])
    trs.push(mbcol[:, :], mbrow[:, :], wait_tok=tk2)
    tk3 = trs.flush(con.ident_f32[0:16, :])
    tk3 = ctx.sop("act", "activation", mbrow_bf[:, :], mbrow[:, :],
                  AF.Identity, deps=[tk3])
    ctx.wtok("pe", tk3)
    r = ctx.op("pe", "matmul", psb[:, 0:16], con.ones_bf[0:1, 0:128],
               mbrow_bf[0:1, :], start=True, stop=True)
    ctx.inc(r, "pe")
    r = ctx.op("pe", "matmul", psb[:, 16:32], con.ones_bf[0:1, 0:128],
               mbrow_bf[0:1, :], start=True, stop=True)
    c_b = ctx.inc(r, "pe")
    ctx.w("dve", "pe", c_b)
    r = ctx.op("dve", "tensor_copy", negmb[:, :], psb[:, 0:16])
    c_nb = ctx.inc(r, "dve")
    ses.close()
    staged["l2_vphbuf"] = vphbuf
    staged["l2_vphbuf_tok"] = {"ds": c_vout}
    staged["l2_negmb"] = negmb
    staged["l2_negmb_tok"] = {"dve": c_nb}


def declare_attn_inputs(ctx, layer):
    if layer == 0:
        pre = "L0_attn_"
        shapes = dict(Wq=[16, 1024, 64], Wk=[16, 1024, 64], Wv=[16, 1024, 64],
                      Aq=[16, 1024, 8], Bq=[16, 8, 64], Ak=[16, 1024, 8],
                      Bk=[16, 8, 64], Av=[16, 1024, 8], Bv=[16, 8, 64],
                      Wp=[1024, 1024], bp=[1024], Ap=[1024, 8], Bp=[8, 1024])
    elif layer == 1:
        pre = "L1_attn_"
        shapes = dict(Wq_w=[64, 64], Wq_b=[64], Wk_w=[64, 64], Wk_b=[64],
                      Wv_w=[64, 64], Wv_b=[64], proj_w=[1024, 1024],
                      proj_b=[1024])
    elif layer == 2:
        pre = "L2_attn_"
        shapes = dict(qkv_w=[3072, 1024], qkv_b=[3072], proj_w=[1024, 1024],
                      proj_b=[1024], rel_pos_emb=[2047, 1024])
    else:
        pre = "L3_attn_"
        shapes = dict(w_attn=[3072, 1024], b_attn=[3072], w_proj=[1024, 1024],
                      b_proj=[1024])
    return {k: ctx.hbm(pre + k, v) for k, v in shapes.items()}



def col_ap_off(vec_ap, off, n):
    return bass.AP(vec_ap.tensor, vec_ap.offset + off, [[1, n], [1, 1]])


def row_ap_off(vec_ap, off, n):
    return bass.AP(vec_ap.tensor, vec_ap.offset + off, [[n, 1], [1, n]])


def emit_qkv_l23(ctx, P, prep, grp, qt4, kt4, vhat, xn, xn_tok, con, rQ):
    wTh = prep["wTh"]
    wTr = wTh.rearrange("(c p) o -> p c o", p=128)
    ring = prep["wring"]
    lag = Lag()
    for pair in range(4):
        gp = grp * 4 + pair
        if len(ring) == 3:
            sl = [0, 1, 2]
        else:
            sl = [(pair % 2) * 3 + j for j in range(3)]
        cds = 0
        for j, off in ((0, gp * 128), (1, 1024 + gp * 128),
                       (2, 2048 + gp * 128)):
            w0 = dict(prep["wring_free"][sl[j]])
            w0.update(prep["wT_tok"])
            cds = ctx.dma("sp", ring[sl[j]][:, :, :],
                          wTr[:, :, off:off + 128], wait=w0)
        for j, dst, scale, bias in (
                (0, qt4[pair], 0.125, prep["qb"]),
                (1, kt4[pair], 1.0, prep["kb"])):
            for tt in range(4):
                kq, pq, frq = rQ.next()
                ctx.wtok("pe", frq)
                ctx.w("pe", "ds", cds)
                ctx.wtok("pe", xn_tok)
                for c in range(NCH):
                    r = ctx.op("pe", "matmul", pq[:, :], ring[sl[j]][:, c, :],
                               xn[:, c, tt * 512:(tt + 1) * 512],
                               start=(c == 0), stop=(c == NCH - 1))
                c_mm = ctx.inc(r, "pe")

                def ev(cover, kq=kq, pq=pq, dst=dst, tt=tt, scale=scale,
                       bias=bias, gp=gp):
                    ctx.w("act", "pe", cover)
                    ctx.wtok("act", prep["b_tok"])
                    r2 = ctx.op("act", "activation",
                                dst[:, tt * 512:(tt + 1) * 512], pq[:, :],
                                AF.Identity, scale=scale,
                                bias=bias[:, gp:gp + 1])
                    rQ.set_free(kq, {"act": ctx.inc(r2, "act")})
                lag.step(c_mm, ev)
        for tt in range(NT):
            kq, pq, frq = rQ.next()
            ctx.wtok("pe", frq)
            ctx.w("pe", "ds", cds)
            ctx.wtok("pe", xn_tok)
            for c in range(NCH):
                r = ctx.op("pe", "matmul", pq[:, 0:128],
                           xn[:, c, tt * 128:(tt + 1) * 128],
                           ring[sl[2]][:, c, :],
                           start=(c == 0), stop=False)
            r = ctx.op("pe", "matmul", pq[:, 0:128], con.ones_bf[0:1, 0:128],
                       prep["vb"][0:1, gp, :], start=False, stop=True)
            c_mm = ctx.inc(r, "pe")

            def ev(cover, kq=kq, pq=pq, tt=tt, pair=pair):
                ctx.w("act", "pe", cover)
                h0, h1 = 2 * pair, 2 * pair + 1
                ctx.op("act", "activation",
                       vhat[:, tt, h0 * 65:h0 * 65 + 64],
                       pq[:, 0:64], AF.Identity)
                r2 = ctx.op("act", "activation",
                            vhat[:, tt, h1 * 65:h1 * 65 + 64],
                            pq[:, 64:128], AF.Identity)
                rQ.set_free(kq, {"act": ctx.inc(r2, "act")})
            lag.step(c_mm, ev)
        for j in range(3):
            prep["wring_free"][sl[j]] = {"pe": ctx.cnt["pe"]}
    kq, pq, frq = rQ.next()
    ctx.wtok("pe", frq)
    r = ctx.op("pe", "matmul", pq[:, 0:128], xn[:, 0, 0:128],
               xn[:, 0, 0:128], start=True, stop=True)
    c_cov = ctx.inc(r, "pe")
    rQ.set_free(kq, {"pe": c_cov})
    lag.flush(c_cov)
    return {"act": ctx.cnt["act"]}


def emit_qkv_l1(ctx, P, prep, grp, qt4, kt4, vhat, xn, xn_tok, con, rQ):
    blk = prep["blk"]
    lag = Lag()
    for pair in range(4):
        chunk = grp * 4 + pair
        for j, dst, scale, bcol in (
                (0, qt4[pair], 0.125, prep["qb2"][:, 2:3]),
                (1, kt4[pair], 1.0, prep["qb2"][:, 1:2])):
            for tt in range(4):
                kq, pq, frq = rQ.next()
                ctx.wtok("pe", frq)
                ctx.wtok("pe", prep["blk_tok"])
                ctx.wtok("pe", xn_tok)
                r = ctx.op("pe", "matmul", pq[:, :], blk[:, j, :],
                           xn[:, chunk, tt * 512:(tt + 1) * 512],
                           start=True, stop=True)
                c_mm = ctx.inc(r, "pe")

                def ev(cover, kq=kq, pq=pq, dst=dst, tt=tt, scale=scale,
                       bcol=bcol):
                    ctx.w("act", "pe", cover)
                    ctx.wtok("act", prep["b_tok"])
                    r2 = ctx.op("act", "activation",
                                dst[:, tt * 512:(tt + 1) * 512],
                                pq[:, :], AF.Identity, scale=scale, bias=bcol)
                    rQ.set_free(kq, {"act": ctx.inc(r2, "act")})
                lag.step(c_mm, ev)
        for tt in range(NT):
            kq, pq, frq = rQ.next()
            ctx.wtok("pe", frq)
            ctx.wtok("pe", prep["blk_tok"])
            ctx.wtok("pe", xn_tok)
            ctx.wtok("pe", prep["vb_tok"])
            r = ctx.op("pe", "matmul", pq[:, 0:128],
                       xn[:, chunk, tt * 128:(tt + 1) * 128],
                       blk[:, 2, :], start=True, stop=False)
            r = ctx.op("pe", "matmul", pq[:, 0:128], con.ones_bf[0:1, 0:128],
                       prep["vb2"][0:1, :], start=False, stop=True)
            c_mm = ctx.inc(r, "pe")

            def ev(cover, kq=kq, pq=pq, tt=tt, pair=pair):
                ctx.w("act", "pe", cover)
                h0, h1 = 2 * pair, 2 * pair + 1
                ctx.op("act", "activation",
                       vhat[:, tt, h0 * 65:h0 * 65 + 64],
                       pq[:, 0:64], AF.Identity)
                r2 = ctx.op("act", "activation",
                            vhat[:, tt, h1 * 65:h1 * 65 + 64],
                            pq[:, 64:128], AF.Identity)
                rQ.set_free(kq, {"act": ctx.inc(r2, "act")})
            lag.step(c_mm, ev)
    kq, pq, frq = rQ.next()
    ctx.wtok("pe", frq)
    r = ctx.op("pe", "matmul", pq[:, 0:128], xn[:, 0, 0:128],
               xn[:, 0, 0:128], start=True, stop=True)
    c_cov = ctx.inc(r, "pe")
    rQ.set_free(kq, {"pe": c_cov})
    lag.flush(c_cov)
    return {"act": ctx.cnt["act"]}


def emit_qkv_l0(ctx, P, prep, grp, qt4, kt4, vhat, xn, xn_tok, con, rQ):
    au = prep["au"]
    bd = prep["bd"]
    au_free = prep["au_free"]
    lag = Lag()
    names = ("q", "k", "v")
    Amats = (P["Aq"], P["Ak"], P["Av"])
    for j in range(3):
        w0 = dict(au_free[0])
        arr = Amats[j].rearrange("h (c p) r -> p c h r", p=128)
        c_a = 0
        for cc in range(NCH):
            c_a = ctx.dma("pool", au[:, cc, :, :],
                          arr[:, cc, grp * 8:(grp + 1) * 8, :],
                          wait=(w0 if cc == 0 else None))
        ctx.w("pe", "dg", c_a)
        ctx.wtok("pe", xn_tok)
        for tt in range(4):
            kq, pq, frq = rQ.next()
            ctx.wtok("pe", frq)
            for c in range(NCH):
                r = ctx.op("pe", "matmul", pq[0:64, :],
                           au[:, c, :, :].rearrange("p a b -> p (a b)"),
                           xn[:, c, tt * 512:(tt + 1) * 512],
                           start=(c == 0), stop=(c == NCH - 1))
            c_mm = ctx.inc(r, "pe")

            def ev(cover, kq=kq, pq=pq, j=j, tt=tt):
                ctx.w("act", "pe", cover)
                r2 = ctx.op("act", "activation",
                            prep["u_" + names[j]][:, tt * 512:(tt + 1) * 512],
                            pq[0:64, :], AF.Identity, scale=2.0)
                rQ.set_free(kq, {"act": ctx.inc(r2, "act")})
            lag.step(c_mm, ev)
        au_free[0] = {"pe": ctx.cnt["pe"]}
    kq, pq, frq = rQ.next()
    ctx.wtok("pe", frq)
    r = ctx.op("pe", "matmul", pq[:, 0:128], xn[:, 0, 0:128],
               xn[:, 0, 0:128], start=True, stop=True)
    c_cov = ctx.inc(r, "pe")
    rQ.set_free(kq, {"pe": c_cov})
    lag.flush(c_cov)
    u_tok = {"act": ctx.cnt["act"]}
    Wmats = (P["Wq"], P["Wk"], P["Wv"])
    Bmats = (P["Bq"], P["Bk"], P["Bv"])
    for pair in range(4):
        gh = grp * 8 + 2 * pair
        bd0 = dict(prep["bd_free"][0])
        ctx.wtok("pool", bd0)
        ctx.op("pool", "memset", bd[:, :, :], 0.0)
        c_bd = 0
        for j in range(3):
            for i in range(2):
                hl = 2 * pair + i
                c_bd = ctx.dma("pool",
                               bd[hl * 8:hl * 8 + 8, j, i * 64:i * 64 + 64],
                               Bmats[j][gh + i])
        for j, dst, scale in ((0, qt4[pair], 0.125), (1, kt4[pair], 1.0)):
            kw, wt_, frw = prep["wq_ring"].next()
            warr = Wmats[j].rearrange("h (c p) d -> p c h d", p=128)
            c_w = 0
            for cc in range(NCH):
                c_w = ctx.dma("pool", wt_[:, cc, :, :],
                              warr[:, cc, gh:gh + 2, :],
                              wait=(frw if cc == 0 else None))
            for tt in range(4):
                kq, pq, frq = rQ.next()
                ctx.wtok("pe", frq)
                ctx.w("pe", "dg", max(c_w, c_bd))
                ctx.wtok("pe", xn_tok)
                ctx.wtok("pe", u_tok)
                for c in range(NCH):
                    r = ctx.op("pe", "matmul", pq[:, :],
                               wt_[:, c, :, :].rearrange("p a b -> p (a b)"),
                               xn[:, c, tt * 512:(tt + 1) * 512],
                               start=(c == 0), stop=False)
                r = ctx.op("pe", "matmul", pq[:, :], bd[:, j, :],
                           prep["u_" + names[j]][:, tt * 512:(tt + 1) * 512],
                           start=False, stop=True)
                c_mm = ctx.inc(r, "pe")

                def ev(cover, kq=kq, pq=pq, dst=dst, tt=tt, scale=scale):
                    ctx.w("act", "pe", cover)
                    r2 = ctx.op("act", "activation",
                                dst[:, tt * 512:(tt + 1) * 512],
                                pq[:, :], AF.Identity, scale=scale)
                    rQ.set_free(kq, {"act": ctx.inc(r2, "act")})
                lag.step(c_mm, ev)
            prep["wq_ring"].set_free(kw, {"pe": ctx.cnt["pe"]})
        kw, wt_, frw = prep["wq_ring"].next()
        warr = Wmats[2].rearrange("h (c p) d -> p c h d", p=128)
        c_w = 0
        for cc in range(NCH):
            c_w = ctx.dma("pool", wt_[:, cc, :, :],
                          warr[:, cc, gh:gh + 2, :],
                          wait=(frw if cc == 0 else None))
        for tt in range(NT):
            kq, pq, frq = rQ.next()
            ctx.wtok("pe", frq)
            ctx.w("pe", "dg", max(c_w, c_bd))
            ctx.wtok("pe", xn_tok)
            ctx.wtok("pe", u_tok)
            for c in range(NCH):
                r = ctx.op("pe", "matmul", pq[:, 0:128],
                           xn[:, c, tt * 128:(tt + 1) * 128],
                           wt_[:, c, :, :].rearrange("p a b -> p (a b)"),
                           start=(c == 0), stop=False)
            r = ctx.op("pe", "matmul", pq[:, 0:128],
                       prep["u_v"][:, tt * 128:(tt + 1) * 128], bd[:, 2, :],
                       start=False, stop=True)
            c_mm = ctx.inc(r, "pe")

            def ev(cover, kq=kq, pq=pq, tt=tt, pair=pair):
                ctx.w("act", "pe", cover)
                h0, h1 = 2 * pair, 2 * pair + 1
                ctx.op("act", "activation",
                       vhat[:, tt, h0 * 65:h0 * 65 + 64],
                       pq[:, 0:64], AF.Identity)
                r2 = ctx.op("act", "activation",
                            vhat[:, tt, h1 * 65:h1 * 65 + 64],
                            pq[:, 64:128], AF.Identity)
                rQ.set_free(kq, {"act": ctx.inc(r2, "act")})
            lag.step(c_mm, ev)
        prep["wq_ring"].set_free(kw, {"pe": ctx.cnt["pe"]})
        prep["bd_free"][0] = {"pe": ctx.cnt["pe"]}
    kq, pq, frq = rQ.next()
    ctx.wtok("pe", frq)
    r = ctx.op("pe", "matmul", pq[:, 0:128], xn[:, 0, 0:128],
               xn[:, 0, 0:128], start=True, stop=True)
    c_cov = ctx.inc(r, "pe")
    rQ.set_free(kq, {"pe": c_cov})
    lag.flush(c_cov)
    return {"act": ctx.cnt["act"]}


def emit_attn(ctx, es, layer, P, staged, x_sb, xn, xn_tok, x_tok, con):
    aes = ExitStack()
    ctx.guard_all()
    start = ctx.snap()
    attnout = ctx.sb(aes, "at_out", [128, NCH, T], BF16)
    ao_tok = {"dve": 0, "ds": 0}

    kes = ExitStack()
    prep = {}
    if layer == 0:
        for nm in ("q", "k", "v"):
            prep["u_" + nm] = ctx.sb(kes, f"at_u{nm}", [64, T], BF16)
        prep["au"] = ctx.sb(kes, "at_au", [128, NCH, 8, 8], BF16)
        prep["au_free"] = [dict(start)]
        prep["bd"] = ctx.sb(kes, "at_bd", [64, 3, 128], BF16)
        prep["bd_free"] = [dict(start)]
        prep["wq_ring"] = Ring(ctx, [ctx.sb(kes, f"at_wq{i}",
                                            [128, NCH, 2, 64], BF16)
                                     for i in range(2)], start)
    elif layer == 1:
        blk = ctx.sb(kes, "at_l1blk", [128, 3, 128], BF16)
        ctx.wtok("pool", start)
        ctx.op("pool", "memset", blk[:, :, :], 0.0)
        cb = 0
        qT = staged["l1_qkvT"]
        w0 = dict(start)
        w0.update(staged["l1_qkvT_tok"])
        for j in range(3):
            cb = ctx.dma("sp", blk[0:64, j, 0:64],
                         qT[:, j * 64:(j + 1) * 64], wait=w0)
            cb = ctx.dma("sp", blk[64:128, j, 64:128],
                         qT[:, j * 64:(j + 1) * 64])
        prep["blk"] = blk
        prep["blk_tok"] = {"ds": cb, "pool": ctx.cnt["pool"]}
        qb2 = ctx.sb(kes, "at_l1qb", [128, 4], F32)
        ctx.dma("pool", qb2[0:64, 0:1], col_ap(P["Wq_b"], 64), wait=start)
        ctx.dma("pool", qb2[64:128, 0:1], col_ap(P["Wq_b"], 64))
        ctx.dma("pool", qb2[0:64, 1:2], col_ap(P["Wk_b"], 64))
        c1 = ctx.dma("pool", qb2[64:128, 1:2], col_ap(P["Wk_b"], 64))
        tkb = ctx.sop("dve", "tensor_scalar", qb2[:, 2:3], qb2[:, 0:1],
                      0.125, None, ALU.mult, deps=[{"dg": c1}])
        prep["qb2"] = qb2
        prep["b_tok"] = tkb
        vb2 = ctx.sb(kes, "at_l1vb", [1, 128], BF16)
        ctx.dma("pool", vb2[0:1, 0:64], row_ap(P["Wv_b"], 64))
        c2 = ctx.dma("pool", vb2[0:1, 64:128], row_ap(P["Wv_b"], 64))
        prep["vb2"] = vb2
        prep["vb_tok"] = {"dg": c2}
    else:
        wkey = "l2_qkvT" if layer == 2 else "l3_qkvT"
        prep["wTh"] = staged[wkey]
        prep["wT_tok"] = staged[wkey + "_tok"]
        nw = 3 if layer == 2 else 6
        prep["wring"] = [ctx.sb(kes, f"at_wq{i}", [128, NCH, 128], BF16)
                         for i in range(nw)]
        prep["wring_free"] = [dict(start) for _ in range(nw)]
        bvec = P["qkv_b"] if layer == 2 else P["b_attn"]
        prep["qb"] = ctx.sb(kes, "at_qb", [128, 8], F32)
        prep["kb"] = ctx.sb(kes, "at_kb", [128, 8], F32)
        prep["vb"] = ctx.sb(kes, "at_vb", [1, 8, 128], BF16)
        cq = 0
        for gp in range(8):
            cq = ctx.dma("pool", prep["qb"][:, gp:gp + 1],
                         col_ap_off(bvec, gp * 128, 128),
                         wait=(start if gp == 0 else None))
            cq = ctx.dma("pool", prep["kb"][:, gp:gp + 1],
                         col_ap_off(bvec, 1024 + gp * 128, 128))
            cq = ctx.dma("pool", prep["vb"][0:1, gp, :],
                         row_ap_off(bvec, 2048 + gp * 128, 128))
        tkb = ctx.sop("dve", "tensor_scalar", prep["qb"][:, :],
                      prep["qb"][:, :], 0.125, None, ALU.mult,
                      deps=[{"dg": cq}])
        prep["b_tok"] = tkb
    if layer == 2:
        prep["hank"] = [ctx.sb(kes, "at_hk0", [128, 16, 256], BF16)]
        prep["hank_load"] = [0]
        negmb = staged["l2_negmb"]
        negmb_tok = staged["l2_negmb_tok"]
    else:
        negmb = con.zeros16
        negmb_tok = con.ready

    pes = ExitStack()
    psQ = [ctx.ps(pes, f"at_pq{i}", [128, 512], F32) for i in range(2)]
    psST = [ctx.ps(pes, f"at_ps{i}", [128, 256], F32) for i in range(2)]
    psPV = [ctx.ps(pes, f"at_pv{i}", [65, 256], F32) for i in range(2)]
    psBC = [ctx.ps(pes, f"at_pb{i}", [128, 256], F32) for i in range(2)]
    rQ = Ring(ctx, psQ, start)
    rST = Ring(ctx, psST, start)
    rPV = Ring(ctx, psPV, start)
    rBC = Ring(ctx, psBC, start)

    for grp in range(2):
        ges = ExitStack()
        qt4 = [ctx.sb(ges, f"at_q{i}", [128, T], BF16) for i in range(4)]
        kt4 = [ctx.sb(ges, f"at_k{i}", [128, T], BF16) for i in range(4)]
        vhat = ctx.sb(ges, "at_v", [128, NT, 520], BF16)
        negMh = ctx.sb(ges, "at_mh", [128, 8], F32)
        aodd = ctx.sb(ges, "at_aodd", [64, T], BF16)
        dmy_sb = ctx.sb(ges, "at_dmy", [64, 64], BF16)
        aodd_free = [dict(start)]
        mall = ctx.sb(ges, "at_ma", [128, 8], F32)
        mcols = ctx.sb(ges, "at_mc", [128, 8, 40], F32)
        msb = ctx.sb(ges, "at_msb", [8, 128], F32)
        m8 = ctx.sb(ges, "at_m8", [8, 1], F32)
        mrow = ctx.sb(ges, "at_mr", [1, 8], F32)
        mrowb = ctx.sb(ges, "at_mrb", [1, 8], BF16)
        probs = [ctx.sb(ges, f"at_pr{i}", [128, 256], BF16) for i in range(2)]
        numer = [ctx.sb(ges, f"at_nm{i}", [64, 256], BF16) for i in range(2)]
        invlt = ctx.sb(ges, "at_il", [65, 2, 256], F32)
        invl = [invlt[64:65, 0, :], invlt[64:65, 1, :]]

        rPR = Ring(ctx, probs, start)
        nm_free = [dict(start), dict(start)]
        r = None
        for hl in range(8):
            ctx.wtok("pool", start)
            r = ctx.op("pool", "memset",
                       vhat[:, :, hl * 65 + 64:hl * 65 + 65], 1.0)
        vone_tok = {"pool": ctx.inc(r, "pool")}

        if layer == 0:
            qkv_tok = emit_qkv_l0(ctx, P, prep, grp, qt4, kt4, vhat, xn,
                                  xn_tok, con, rQ)
        elif layer == 1:
            qkv_tok = emit_qkv_l1(ctx, P, prep, grp, qt4, kt4, vhat, xn,
                                  xn_tok, con, rQ)
        else:
            qkv_tok = emit_qkv_l23(ctx, P, prep, grp, qt4, kt4, vhat, xn,
                                   xn_tok, con, rQ)

        ctx.guard_all()
        p1_pend = [None]

        def p1_flush(cover):
            if p1_pend[0] is None:
                return
            kq_, ap_, hl_, j_ = p1_pend[0]
            ctx.w("dve", "pe", cover)
            r2 = ctx.op("dve", "tensor_reduce", mcols[:, hl_, j_:j_ + 1],
                        ap_, AX.X, ALU.max)
            c_r = ctx.inc(r2, "dve")
            rQ.set_free(kq_, {"dve": c_r})
            p1_pend[0] = None

        ncu_by_head = []
        for hl in range(8):
            pair, rb = hl // 2, (hl % 2) * 64
            ncu = 0
            for qt in range(NT):
                slen = (qt + 1) * 128
                for sci in range((slen + 511) // 512):
                    n = min(512, slen - sci * 512)
                    kq, p1, frq = rQ.next()
                    ctx.wtok("pe", frq)
                    ctx.wtok("pe", qkv_tok)
                    r = ctx.op("pe", "matmul", p1[:, 0:n],
                               qt4[pair][rb:rb + 64,
                                         qt * 128:(qt + 1) * 128],
                               kt4[pair][rb:rb + 64,
                                         sci * 512:sci * 512 + n],
                               start=True, stop=True)
                    c_s = ctx.inc(r, "pe")
                    p1_flush(c_s)
                    p1_pend[0] = (kq, p1[:, 0:n], hl, ncu)
                    ncu += 1
            ncu_by_head.append(ncu)
        kq, p1, frq = rQ.next()
        ctx.wtok("pe", frq)
        r = ctx.op("pe", "matmul", p1[:, 0:128], qt4[0][0:64, 0:128],
                   kt4[0][0:64, 0:128], start=True, stop=True)
        c_cov = ctx.inc(r, "pe")
        rQ.set_free(kq, {"pe": c_cov})
        p1_flush(c_cov)
        for hl in range(8):
            ctx.sop("dve", "tensor_reduce", mall[:, hl:hl + 1],
                    mcols[:, hl, 0:ncu_by_head[hl]], AX.X, ALU.max)
        mall_tok = {"dve": ctx.cnt["dve"]}

        k0, bc0, fr0 = rBC.next()
        ctx.wtok("pe", fr0)
        ctx.wtok("pe", mall_tok)
        r = ctx.op("pe", "transpose", bc0[0:8, 0:128], mall[:, :],
                   con.ident_f32[:, :])
        ctx.inc(r, "pe")
        k1, bc1, fr1 = rBC.next()
        ctx.wtok("pe", fr1)
        r = ctx.op("pe", "transpose", bc1[0:8, 0:128], con.ident_f32[:, 0:8],
                   con.ident_f32[:, :])
        c_cov = ctx.inc(r, "pe")
        tk = ctx.sop("dve", "tensor_copy", msb[:, :], bc0[0:8, 0:128],
                     deps=[{"pe": c_cov}])
        rBC.set_free(k0, tk)
        rBC.set_free(k1, {"pe": c_cov})
        tk = ctx.sop("dve", "tensor_reduce", m8[:, :], msb[:, :], AX.X,
                     ALU.max, deps=[tk])
        k0, bc0, fr0 = rBC.next()
        ctx.wtok("pe", fr0)
        ctx.wtok("pe", tk)
        r = ctx.op("pe", "transpose", bc0[0:1, 0:8], m8[:, :],
                   con.ident_f32[0:8, 0:8])
        ctx.inc(r, "pe")
        k1, bc1, fr1 = rBC.next()
        ctx.wtok("pe", fr1)
        r = ctx.op("pe", "transpose", bc1[0:1, 0:8], con.ident_f32[0:8, 0:1],
                   con.ident_f32[0:8, 0:8])
        c_cov = ctx.inc(r, "pe")
        tk = ctx.sop("dve", "tensor_copy", mrow[:, :], bc0[0:1, 0:8],
                     deps=[{"pe": c_cov}])
        rBC.set_free(k0, tk)
        rBC.set_free(k1, {"pe": c_cov})
        tk = ctx.sop("act", "activation", mrowb[:, :], mrow[:, :],
                     AF.Identity, deps=[tk])
        k0, bc0, fr0 = rBC.next()
        ctx.wtok("pe", fr0)
        ctx.wtok("pe", tk)
        r = ctx.op("pe", "matmul", bc0[:, 0:8], con.ones_bf[0:1, 0:128],
                   mrowb[0:1, :], start=True, stop=True)
        ctx.inc(r, "pe")
        k1, bc1, fr1 = rBC.next()
        ctx.wtok("pe", fr1)
        r = ctx.op("pe", "matmul", bc1[:, 0:8], con.ones_bf[0:1, 0:128],
                   mrowb[0:1, :], start=True, stop=True)
        c_cov = ctx.inc(r, "pe")
        ctx.wtok("dve", negmb_tok)
        mh_tok = ctx.sop("dve", "scalar_tensor_tensor", negMh[:, :],
                         bc0[:, 0:8], -1.0, negmb[:, grp * 8:grp * 8 + 8],
                         ALU.mult, ALU.add, deps=[{"pe": c_cov}])
        rBC.set_free(k0, mh_tok)
        rBC.set_free(k1, {"pe": c_cov})

        ctx.guard_all()
        def emit_tail(tail, cover):
            nonlocal ao_tok
            hl, qg, kpv, pvp, c_pv, knm = tail
            gh = grp * 8 + hl
            num_sb, il_sb = numer[knm], invl[knm]
            ctx.wtok("dve", nm_free[knm])
            ctx.w("dve", "pe", cover)
            r = ctx.op("dve", "reciprocal", il_sb[:, :], pvp[64:65, :])
            c_il = ctx.inc(r, "dve")
            ctx.wtok("act", nm_free[knm])
            ctx.w("act", "pe", cover)
            r = ctx.op("act", "activation", num_sb[:, :], pvp[0:64, :],
                       AF.Identity)
            c_nu = ctx.inc(r, "act")
            rPV.set_free(kpv, {"dve": c_il, "act": c_nu})
            kb_, bcb, frb = rBC.next()
            ctx.wtok("pe", frb)
            ctx.w("pe", "dve", c_il)
            r = ctx.op("pe", "matmul", bcb[0:64, :],
                       con.ones_hi[64:65, :], il_sb[0:1, :],
                       start=True, stop=True)
            ctx.inc(r, "pe")
            kb2_, bcb2, frb2 = rBC.next()
            ctx.wtok("pe", frb2)
            r = ctx.op("pe", "matmul", bcb2[0:64, :],
                       con.ones_hi[64:65, :], il_sb[0:1, :],
                       start=True, stop=True)
            c_b = ctx.inc(r, "pe")
            rBC.set_free(kb2_, {"pe": c_b})
            ctx.w("dve", "pe", c_b)
            ctx.w("dve", "act", c_nu)
            if gh % 2 == 0:
                dst = attnout[0:64, gh // 2, qg * 256:(qg + 1) * 256]
            else:
                ctx.wtok("dve", aodd_free[0])
                dst = aodd[:, qg * 256:(qg + 1) * 256]
            r = ctx.op("dve", "tensor_tensor", dst, num_sb[:, :],
                       bcb[0:64, :], ALU.mult)
            c_ao = ctx.inc(r, "dve")
            if gh % 2 == 1 and qg == 7:
                c_mv = ctx.dma("sp", attnout[64:128, gh // 2, :], aodd[:, :],
                               wait={"dve": c_ao})
                c_mv = ctx.dma("sp", dmy_sb[:, :], aodd[:, 0:64])
                aodd_free[0] = {"ds": c_mv}
                ao_tok["ds"] = c_mv
            rBC.set_free(kb_, {"dve": c_ao})
            nm_free[knm] = {"dve": c_ao}
            ao_tok = {"dve": max(ao_tok.get("dve", 0), c_ao),
                      "ds": ao_tok.get("ds", 0)}

        nmi = 0
        seq = []
        for hl in range(8):
            for qg in range(8):
                for c in range(2 * qg + 2):
                    seq.append((hl, qg, c))
        pend_exp = [None]
        pv_state = {}

        def p2_flush(cover):
            if pend_exp[0] is None:
                return
            hl, qg, c, kst, pst_ap = pend_exp[0]
            pend_exp[0] = None
            nch = 2 * qg + 2
            kpr, prb, frp = rPR.next()
            ctx.wtok("act", frp)
            ctx.w("act", "pe", cover)
            ctx.wtok("act", mh_tok)
            r2 = ctx.op("act", "activation", prb[:, :], pst_ap, AF.Exp,
                        bias=negMh[:, hl:hl + 1])
            c_e = ctx.inc(r2, "act")
            rST.set_free(kst, {"act": c_e})
            c_mask = None
            if c == 2 * qg:
                ctx.w("dve", "act", c_e)
                r2 = ctx.op("dve", "tensor_mul", prb[:, 0:128],
                            prb[:, 0:128], con.tri[:, :])
                c_mask = ctx.inc(r2, "dve")
            elif c == 2 * qg + 1:
                ctx.w("dve", "act", c_e)
                ctx.op("dve", "memset", prb[:, 0:128], 0.0)
                r2 = ctx.op("dve", "tensor_mul", prb[:, 128:256],
                            prb[:, 128:256], con.tri[:, :])
                c_mask = ctx.inc(r2, "dve")
            if c == 0:
                kpv, pvp, frv = rPV.next()
                ctx.wtok("pe", frv)
                pv_state[(hl, qg)] = (kpv, pvp)
            kpv, pvp = pv_state[(hl, qg)]
            ctx.w("pe", "act", c_e)
            if c_mask:
                ctx.w("pe", "dve", c_mask)
            ctx.wtok("pe", vone_tok)
            r2 = ctx.op("pe", "matmul", pvp[:, :],
                        vhat[:, c, hl * 65:hl * 65 + 65], prb[:, :],
                        start=(c == 0), stop=(c == nch - 1),
                        skip_group_check=True)
            c_pv = ctx.inc(r2, "pe")
            rPR.set_free(kpr, {"pe": c_pv})
            if c == nch - 1:
                pend_tails.append((hl, qg, kpv, pvp, c_pv, nmi[0] % 2))
                nmi[0] += 1

        nmi = [0]
        pend_tails = []
        if layer == 2:
            gh0 = grp * 8
            src0 = bass.AP(staged["l2_vphbuf"].tensor, gh0 * 2432 + 1,
                           [[1, 128], [128, 16], [1, 256]])
            w0 = dict(start)
            w0.update(staged["l2_vphbuf_tok"])
            c_h = ctx.dma("pool", prep["hank"][0][:, :, :], src0, wait=w0)
            prep["hank_load"][0] = c_h
        cur_hl = -1
        for (hl, qg, c) in seq:
            if hl != cur_hl:
                cur_hl = hl
                if layer == 2 and hl >= 1:
                    gh2 = grp * 8 + hl
                    src0 = bass.AP(staged["l2_vphbuf"].tensor,
                                   gh2 * 2432 + 1,
                                   [[1, 128], [128, 16], [1, 256]])
                    c_h = ctx.dma("pool", prep["hank"][0][:, :, :], src0,
                                  wait={"pe": ctx.cnt["pe"]})
                    prep["hank_load"][0] = c_h
                if layer == 2:
                    ctx.w("pe", "dg", prep["hank_load"][0])
            pair, rb = hl // 2, (hl % 2) * 64
            kst, pst_, frst = rST.next()
            ctx.wtok("pe", frst)
            r = ctx.op("pe", "matmul", pst_[:, :],
                       kt4[pair][rb:rb + 64, c * 128:(c + 1) * 128],
                       qt4[pair][rb:rb + 64, qg * 256:(qg + 1) * 256],
                       start=True, stop=(layer != 2))
            if layer == 2:
                didx = 2 * qg - c + 1
                r = ctx.op("pe", "matmul", pst_[:, :], con.jrev[:, :],
                           prep["hank"][0][:, didx, :],
                           start=False, stop=True)
            c_st = ctx.inc(r, "pe")
            while pend_tails:
                emit_tail(pend_tails.pop(0), c_st)
            p2_flush(c_st)
            pend_exp[0] = (hl, qg, c, kst, pst_[:, :])
        for _ in range(2):
            kst, pst_, frst = rST.next()
            ctx.wtok("pe", frst)
            r = ctx.op("pe", "matmul", pst_[:, :],
                       kt4[0][0:64, 0:128], qt4[0][0:64, 0:256],
                       start=True, stop=True)
            c_cov = ctx.inc(r, "pe")
            rST.set_free(kst, {"pe": c_cov})
            while pend_tails:
                emit_tail(pend_tails.pop(0), c_cov)
            p2_flush(c_cov)
        ctx.guard_all()
        ges.close()

    ctx.dbg(f"attnout{layer}", attnout[:, :, :], ao_tok)
    pres = ExitStack()
    ctx.guard_all()
    pstart = ctx.snap()
    wp = ctx.sb(pres, "at_wp", [128, NCH, 1024], BF16)
    brow = ctx.sb(pres, "at_bp", [1, 1024], BF16)
    if layer == 0:
        c_wp = ctx.dma("pool", wp[:, :, :],
                       P["Wp"].rearrange("(c p) o -> p c o", p=128),
                       wait=pstart)
        c_bp = ctx.dma("pool", brow[:, :], row_ap(P["bp"], 1024))
        apb = ctx.sb(pres, "at_ap", [128, NCH, 8], BF16)
        bpb = ctx.sb(pres, "at_bpb", [8, 1024], BF16)
        tfm = ctx.sb(pres, "at_tfm", [8, T], BF16)
        c_ap = ctx.dma("pool", apb[:, :, :],
                       P["Ap"].rearrange("(c p) r -> p c r", p=128))
        c_bpb = ctx.dma("pool", bpb[:, :], P["Bp"])
        ctx.w("pe", "dg", c_ap)
        ctx.wtok("pe", ao_tok)
        for _d in range(6):
            kq, pq, frq = rQ.next()
            ctx.wtok("pe", frq)
            r = ctx.op("pe", "matmul", pq[:, :], attnout[:, 0, 0:128],
                       attnout[:, 0, 0:512], start=True, stop=True)
            rQ.set_free(kq, {"pe": ctx.inc(r, "pe")})
        for tt in range(4):
            kq, pq, frq = rQ.next()
            ctx.wtok("pe", frq)
            for c in range(NCH):
                r = ctx.op("pe", "matmul", pq[0:8, :],
                           apb[:, c, :],
                           attnout[:, c, tt * 512:(tt + 1) * 512],
                           start=(c == 0), stop=(c == NCH - 1))
            ctx.inc(r, "pe")
            kq2, pq2, frq2 = rQ.next()
            ctx.wtok("pe", frq2)
            r = ctx.op("pe", "matmul", pq2[:, :], attnout[:, 0, 0:128],
                       attnout[:, 0, 0:512], start=True, stop=True)
            c_cov = ctx.inc(r, "pe")
            rQ.set_free(kq2, {"pe": c_cov})
            ctx.w("act", "pe", c_cov)
            r = ctx.op("act", "activation", tfm[:, tt * 512:(tt + 1) * 512],
                       pq[0:8, :], AF.Identity, scale=2.0)
            rQ.set_free(kq, {"act": ctx.inc(r, "act")})
        tfm_tok = {"act": ctx.cnt["act"], "dg": c_bpb}
        ctx.dbg("tfm0", tfm[:, :], tfm_tok)
    else:
        pk = {1: "l1_projT", 2: "l2_projT", 3: "l3_projT"}[layer]
        w0 = dict(pstart)
        w0.update(staged[pk + "_tok"])
        c_wp = ctx.dma("sp", wp[:, :, :],
                       staged[pk].rearrange("(c p) o -> p c o", p=128),
                       wait=w0)
        bsrc = P["b_proj"] if layer == 3 else P["proj_b"]
        c_bp = ctx.dma("pool", brow[:, :], row_ap(bsrc, 1024), wait=pstart)
        tfm_tok = None
    x_last = [dict(x_tok)]
    plag = Lag()
    for tt in range(NT):
        for do in range(2):
            kq, pq, frq = rQ.next()
            ctx.wtok("pe", frq)
            ctx.w("pe", "dg" if layer == 0 else "ds", c_wp)
            ctx.w("pe", "dg", c_bp)
            ctx.wtok("pe", ao_tok)
            for c in range(NCH):
                r = ctx.op("pe", "matmul", pq[:, :],
                           attnout[:, c, tt * 128:(tt + 1) * 128],
                           wp[:, c, do * 512:(do + 1) * 512],
                           start=(c == 0), stop=False)
            r = ctx.op("pe", "matmul", pq[:, :], con.ones_bf[0:1, 0:128],
                       brow[0:1, do * 512:(do + 1) * 512],
                       start=False, stop=(layer != 0))
            if layer == 0:
                ctx.wtok("pe", tfm_tok)
                r = ctx.op("pe", "matmul", pq[:, :],
                           tfm[:, tt * 128:(tt + 1) * 128],
                           bpb[:, do * 512:(do + 1) * 512],
                           start=False, stop=True)
            c_mm = ctx.inc(r, "pe")

            def ev(cover, kq=kq, pq=pq, tt=tt, do=do):
                ctx.w("dve", "pe", cover)
                xs = x_sb[:, tt, do * 512:(do + 1) * 512]
                r2 = ctx.op("dve", "scalar_tensor_tensor", xs, pq[:, :], 1.0,
                            xs, ALU.mult, ALU.add)
                c_x = ctx.inc(r2, "dve")
                rQ.set_free(kq, {"dve": c_x})
                x_last[0] = {"dve": c_x}
            plag.step(c_mm, ev)
    kq, pq, frq = rQ.next()
    ctx.wtok("pe", frq)
    r = ctx.op("pe", "matmul", pq[:, :], attnout[:, 0, 0:128],
               attnout[:, 0, 0:512], start=True, stop=True)
    c_cov = ctx.inc(r, "pe")
    rQ.set_free(kq, {"pe": c_cov})
    plag.flush(c_cov)
    x_last = x_last[0]
    pres.close()
    pes.close()
    kes.close()
    aes.close()
    return x_last



def emit_stage_l3_router(ctx, es, P, con, staged):
    rt = ctx.scratch_hbm("l3_routerT_s", [1024, 16], BF16)
    ctx.guard_all()
    ses = ExitStack()
    rsb = ctx.sb(ses, "st_rw", [16, 1024], BF16)
    rtb = ctx.sb(ses, "st_rt", [128, NCH, 16], BF16)
    pt = [ctx.ps(ses, f"st_pt{i}", [128, 16], BF16) for i in range(2)]
    ptr = Ring(ctx, pt, ctx.snap())
    c_in = ctx.dma("pool", rsb[:, :], P["router_w"])
    ctx.dbg("st_rsb", rsb[:, :], {"dg": c_in})
    ctx.w("pe", "dg", c_in)
    ctx.wtok("pe", con.ready)
    trs = TrStream(ctx, ptr, con.ident_bf, con.ident_f32)
    for c in range(NCH):
        trs.push(rsb[:, c * 128:(c + 1) * 128], rtb[:, c, :])
    tok = trs.flush(con.ident_bf[0:16, :])
    ctx.dbg("st_rtb", rtb[:, :, :], tok)
    ctx.w("sp", "dve", tok["dve"])
    c_out = ctx.dma("sp", rt.rearrange("(c p) e -> p c e", p=128),
                    rtb[:, :, :])
    staged["l3_routerT"] = rt
    staged["l3_routerT_tok"] = {"ds": c_out}
    ses.close()



def declare_layer_inputs(ctx, layer):
    P = {}
    if layer == 0:
        pre = "L0_ffn_"
        shapes = dict(gate_w=[1024, 4], gate_b=[4], sw1=[2, 1024, 4096],
                      sb1=[2, 4096], sw2=[2, 4096, 1024], sb2=[2, 1024],
                      rw1=[4, 1024, 4096], rb1=[4, 4096],
                      rw2=[4, 4096, 1024], rb2=[4, 1024])
        P["ffn"] = {k: ctx.hbm(pre + k, v) for k, v in shapes.items()}
    elif layer == 2:
        pre = "L2_ffn_"
        shapes = dict(gate_w=[1024, 4], gate_b=[4], sw1=[2, 1024, 4096],
                      sb1=[2, 4096], sw2=[2, 4096, 1024], sb2=[2, 1024],
                      rw1=[4, 1024, 4096], rb1=[4, 4096],
                      rw2=[4, 4096, 1024], rb2=[4, 1024])
        P["ffn"] = {k: ctx.hbm(pre + k, v) for k, v in shapes.items()}
    elif layer == 1:
        pre = "L1_ffn_"
        shapes = dict(Wg=[1024, 8], Ws1=[1024, 2048], Ws3=[1024, 2048],
                      Ws2=[2048, 1024], We1=[8, 1024, 2048],
                      We3=[8, 1024, 2048], We2=[8, 2048, 1024])
        P["ffn"] = {k: ctx.hbm(pre + k, v) for k, v in shapes.items()}
    elif layer == 3:
        pre = "L3_ffn_"
        shapes = dict(router_w=[16, 1024], router_b=[16],
                      expert_w=[16, 1024, 1024])
        P["ffn"] = {k: ctx.hbm(pre + k, v) for k, v in shapes.items()}
    return P


def build_program(layers, nsub=None, dbg=None):
    nc = bass.Bass("TRN2", target_bir_lowering=False)
    ctx = Ctx(nc)
    es = ctx.es
    x_in = ctx.hbm("x_in", [T, D], F32)
    x_out = ctx.hbm("x_out", [T, D], F32, kind="ExternalOutput")

    for s in ("pe", "act", "dve", "pool", "dg", "ds", "st"):
        ctx.sems[s] = es.enter_context(nc.semaphore(s))

    x_sb = ctx.sb(es, "x_sb", [128, NT, D], F32)
    xn = ctx.sb(es, "xn", [128, NCH, T], BF16)
    con = emit_consts(ctx, es, need_l2=(2 in layers))

    c_x = ctx.dma("sp", x_sb[:, :, :],
                  x_in.rearrange("(t p) d -> p t d", p=128))
    x_tok = {"ds": c_x}

    staged = {}
    PA = {}
    for layer in layers:
        PA[layer] = declare_attn_inputs(ctx, layer)
    if 3 in layers:
        P3 = declare_layer_inputs(ctx, 3)
        emit_stage_l3_router(ctx, es, P3["ffn"], con, staged)
    if 1 in layers:
        emit_stage_l1_qkvT(ctx, es, PA[1], con, staged)
        emit_stage_wT(ctx, es, "l1_projT", PA[1]["proj_w"], 1024, 1024, con,
                      staged)
    if 2 in layers:
        emit_stage_wT(ctx, es, "l2_qkvT", PA[2]["qkv_w"], 3072, 1024, con,
                      staged)
        emit_stage_wT(ctx, es, "l2_projT", PA[2]["proj_w"], 1024, 1024, con,
                      staged)
        emit_stage_l2_table(ctx, es, PA[2], con, staged)
    if 3 in layers:
        emit_stage_wT(ctx, es, "l3_qkvT", PA[3]["w_attn"], 3072, 1024, con,
                      staged)
        emit_stage_wT(ctx, es, "l3_projT", PA[3]["w_proj"], 1024, 1024, con,
                      staged)

    nsub_done = 0
    for layer in layers:
        P = declare_layer_inputs(ctx, layer)
        for sub in range(3):
            if nsub is not None and nsub_done >= nsub:
                break
            xn_tok = emit_ln(ctx, es, x_sb, xn, con, x_tok)
            if dbg == "xn1" and nsub_done == 0:
                dxn = ctx.hbm("dbg_xn", [128, NCH, T], BF16,
                              kind="ExternalOutput")
                ctx.dma("sp", dxn, xn[:, :, :], wait=xn_tok)
            if sub == 1:
                if SKIP_ATTN:
                    continue
                x_tok = emit_attn(ctx, es, layer, PA[layer], staged, x_sb,
                                  xn, xn_tok, x_tok, con)
            elif layer in (0, 2):
                x_tok = emit_ffn_dense(ctx, es, P["ffn"], x_sb, xn, xn_tok,
                                       x_tok, con)
            elif layer == 1:
                x_tok = emit_ffn_l1(ctx, es, P["ffn"], x_sb, xn, xn_tok,
                                    x_tok, con)
            else:
                x_tok = emit_ffn_l3(ctx, es, P["ffn"], x_sb, xn, xn_tok,
                                    x_tok, con, staged)
            nsub_done += 1

    c_st = ctx.dma("sp", x_out.rearrange("(t p) d -> p t d", p=128),
                   x_sb[:, :, :], wait=x_tok)
    ctx.w("sp", "ds", c_st)

    replay(ctx)
    return nc


def replay(ctx):
    nc = ctx.nc
    with ctx.es:
        block = ctx.es.enter_context(nc.Block())
        emap = {}

        def mk(eng_name):
            def body(e):
                sems = ctx.sems
                for r in ctx.ops[eng_name]:
                    if r.meth == "wait_ge":
                        e.wait_ge(sems[r.args[0]], r.args[1])
                        continue
                    try:
                        ins = getattr(e, r.meth)(*r.args, **r.kwargs)
                    except Exception:
                        import sys
                        print(f"REPLAY FAIL {eng_name} {r.meth} "
                              f"args={[getattr(a, 'shape', a) for a in r.args]}",
                              file=sys.stderr)
                        raise
                    for s, k in r.incs:
                        ins.then_inc(sems[s], k)
            return body
        block.tensor(mk("pe"))
        block.scalar(mk("act"))
        block.vector(mk("dve"))
        block.gpsimd(mk("pool"))
        block.sync(mk("sp"))



def _shard_inputs(inputs, layers):
    names = set()
    pre = {0: ["L0_ffn_", "L0_attn_"], 1: ["L1_ffn_", "L1_attn_"],
           2: ["L2_ffn_", "L2_attn_"], 3: ["L3_ffn_", "L3_attn_"]}
    for l in layers:
        for p in pre[l]:
            names.update(k for k in inputs if k.startswith(p))
    names.discard("L0_attn_x")
    base = {k: np.ascontiguousarray(np.asarray(inputs[k], np.float32))
            for k in names}
    return base


def _run(layers, x, inputs):
    nc = build_program(layers)
    base = _shard_inputs(inputs, layers)
    in_maps = []
    for b in range(8):
        m = dict(base)
        m["x_in"] = np.ascontiguousarray(x[b])
        in_maps.append(m)
    res = run_bass_kernel_spmd(nc, in_maps, core_ids=list(range(8)))
    return np.stack([r["x_out"] for r in res.results], axis=0)


def kernel(**inputs):
    x = np.ascontiguousarray(np.asarray(inputs["L0_attn_x"], np.float32))
    if MODE == "fused":
        x = _run([0, 1, 2, 3], x, inputs)
    else:
        for layer in range(4):
            x = _run([layer], x, inputs)
    return x


